# version 47
# baseline (speedup 1.0000x reference)
import numpy as np
import ml_dtypes
import concourse.bass as bass
import concourse.mybir as mybir
from concourse.bass_utils import run_bass_kernel_spmd

F32 = mybir.dt.float32
BF16 = mybir.dt.bfloat16
AF = mybir.ActivationFunctionType
ALU = mybir.AluOpType

NCORES = 8
NTOK = 4096
DM = 1024
NT = 8
T = 512
S = 4
C = 8
EPS = 1e-5

ENGINES = ("pe", "act", "dve", "pool", "sp")


class _Buf:
    __slots__ = ("name", "writer", "readers")

    def __init__(self, name):
        self.name = name
        self.writer = None
        self.readers = []


class Sched:
    def __init__(self):
        self.ops = {e: [] for e in ENGINES}
        self.dma_cnt = {}
        self.need = {e: set() for e in ENGINES}

    def buf(self, name):
        return _Buf(name)

    def bufs(self, name, n=C):
        return [_Buf(f"{name}{i}") for i in range(n)]

    def _waits_for(self, eng, reads, writes):
        ew = {}
        dw = {}

        def add(w):
            if w[0] == "E":
                if w[1] != eng and ew.get(w[1], -1) < w[2]:
                    ew[w[1]] = w[2]
            else:
                if dw.get(w[1], 0) < w[2]:
                    dw[w[1]] = w[2]

        for b in reads:
            if b.writer is not None:
                add(b.writer)
        for b in writes:
            if b.writer is not None:
                add(b.writer)
            for r in b.readers:
                add(r)
        return ([("E", k, v) for k, v in ew.items()]
                + [("D", k, v) for k, v in dw.items()])

    def op(self, eng, emit, reads=(), writes=()):
        reads = tuple(reads)
        writes = tuple(writes)
        idx = len(self.ops[eng])
        waits = self._waits_for(eng, reads, writes)
        self.ops[eng].append({"emit": emit, "waits": waits, "dma": None})
        me = ("E", eng, idx)
        for b in reads:
            b.readers.append(me)
        for b in writes:
            b.writer = me
            b.readers = []
        return idx

    def dma(self, queue, sem, emit, reads=(), writes=()):
        reads = tuple(reads)
        writes = tuple(writes)
        waits = self._waits_for(queue, reads, writes)
        val = self.dma_cnt.get(sem, 0) + 16
        self.dma_cnt[sem] = val
        self.ops[queue].append({"emit": emit, "waits": waits, "dma": (sem, val)})
        me = ("D", sem, val)
        for b in reads:
            b.readers.append(me)
        for b in writes:
            b.writer = me
            b.readers = []

    def finalize(self):
        for e in ENGINES:
            for o in self.ops[e]:
                for w in o["waits"]:
                    if w[0] == "E":
                        self.need[w[1]].add(w[2])
        self.valmap = {}
        for e in ENGINES:
            idxs = sorted(self.need[e])
            self.valmap[e] = {ix: r + 1 for r, ix in enumerate(idxs)}

    def emit_engine(self, eng_name, eng, esems, dsems):
        waited = {}
        vm = self.valmap
        my_need = self.need[eng_name]
        for i, o in enumerate(self.ops[eng_name]):
            merged = {}
            for w in o["waits"]:
                if w[0] == "E":
                    key = ("E", w[1])
                    val = vm[w[1]][w[2]]
                else:
                    key = ("D", w[1])
                    val = w[2]
                if merged.get(key, 0) < val:
                    merged[key] = val
            for key, val in merged.items():
                if waited.get(key, 0) >= val:
                    continue
                waited[key] = val
                sem = esems[key[1]] if key[0] == "E" else dsems[key[1]]
                eng.wait_ge(sem, val)
            inst = o["emit"](eng)
            if o["dma"] is not None:
                inst.then_inc(dsems[o["dma"][0]], 16)
            elif i in my_need:
                inst.then_inc(esems[eng_name], 1)


def build_program():
    nc = bass.Bass("TRN2", target_bir_lowering=False)

    xT = nc.dram_tensor("xT", [DM, NTOK], F32, kind="ExternalInput")
    outT = nc.dram_tensor("outT", [DM, NTOK], F32, kind="ExternalOutput")
    w_h = {n: nc.dram_tensor(n, [DM, DM], F32, kind="ExternalInput")
           for n in ("wk", "wv", "wq", "wh", "wo", "wf")}
    biasc_h = nc.dram_tensor("biasc", [128, 56], F32, kind="ExternalInput")
    browk_h = nc.dram_tensor("browk", [1, DM], F32, kind="ExternalInput")
    browv_h = nc.dram_tensor("browv", [1, DM], F32, kind="ExternalInput")

    sch = Sched()
    B = sch.buf
    import contextlib
    ctx = contextlib.ExitStack()

    def sb(name, shape, dt):
        return ctx.enter_context(nc.sbuf_tensor(name, shape, dt))

    def ps(name, shape, dt):
        return ctx.enter_context(nc.psum_tensor(name, shape, dt))

    wsl = [sb(f"w{i}", [128, C, DM], BF16) for i in range(4)]
    wbuf = [B(f"w{i}") for i in range(4)]

    biasc = sb("biascs", [128, 56], F32); b_biasc = B("biasc")
    bhb = sb("bhb", [128, C], F32); b_bhb = B("bhb")
    browkb = sb("browkb", [1, DM], BF16); b_browkb = B("browkb")
    browvb = sb("browvb", [1, DM], BF16); b_browvb = B("browvb")
    ones = sb("ones", [128, 128], BF16); b_ones = B("ones")
    epsc = sb("epsc", [128, 1], F32); b_epsc = B("epsc")
    neghalf = sb("neghalf", [128, 1], F32); b_neghalf = B("neghalf")
    negone = sb("negone", [128, 1], F32); b_negone = B("negone")
    zeroc = sb("zeroc", [128, 1], F32); b_zeroc = B("zeroc")
    junkt = sb("junkt", [128, T], BF16); b_junkt = B("junkt")
    e2m = sb("e2m", [16, C, 128], BF16); b_e2m = B("e2m")
    ltden = sb("ltden", [128, C, 16], BF16); b_ltden = B("ltden")
    dinv16 = sb("dinv16", [16, T], BF16); b_dinv16 = B("dinv16")

    A1 = sb("A1", [128, NT, T], BF16); b_A1 = B("A1")
    B1 = sb("B1", [128, NT, T], BF16); b_B1 = B("B1")
    xbf = [sb(f"xbf{i}", [128, C, T], BF16) for i in range(2)]
    b_xbf = [B("xbf0"), B("xbf1")]

    xns = [sb("xn0", [128, C, T], BF16), sb("xn1", [128, C, T], BF16)]
    b_xns = [sch.bufs("xn0"), sch.bufs("xn1")]
    qs = sb("qs", [128, C, T], BF16); b_q = sch.bufs("q")
    hs = sb("hs", [128, C, T], BF16); b_h = sch.bufs("h")
    nsb = sb("nsb", [128, C, T], BF16); b_nsb = sch.bufs("nsb")
    s1 = sb("s1", [128, C, T], BF16); b_s1 = sch.bufs("s1")
    s2 = sb("s2", [128, C, T], BF16); b_s2 = sch.bufs("s2")
    YY = sb("YY", [128, C, T], BF16); b_YY = sch.bufs("YY")

    kv_sb = sb("kv_sb", [128, C, 64], BF16); b_kv = B("kv_sb")
    kvblk = sb("kvblk", [128, C, 128], BF16); b_kvblk = B("kvblk")
    ksum_sb = sb("ksum_sb", [128, C], F32); b_ksum = B("ksum_sb")
    bvband = sb("bvband", [128, C, 64], BF16); b_bvband = B("bvband")

    reps = {n: sb(n, [128, T], BF16) for n in
            ("A2", "B2", "A3", "B3", "A4", "B4", "A5", "B5", "C5")}
    b_reps = {n: B(n) for n in reps}
    m1 = sb("m1", [128, T], F32); b_m1 = B("m1")
    m2 = sb("m2", [128, T], F32); b_m2 = B("m2")
    m3 = sb("m3", [128, T], F32); b_m3 = B("m3")
    tb1 = sb("tb1", [128, T], BF16); b_tb1 = B("tb1")
    tb2 = sb("tb2", [128, T], BF16); b_tb2 = B("tb2")
    tp1 = sb("tp1", [128, T], BF16); b_tp1 = B("tp1")
    tp2 = sb("tp2", [128, T], BF16); b_tp2 = B("tp2")
    out_st = sb("out_st", [128, 4, T], F32); b_outs = sch.bufs("outst", 4)

    pk = [ps(f"pk{i}", [128, 512], F32) for i in range(8)]
    b_pk = [B(f"pk{i}") for i in range(8)]
    PA = [0, 1, 2, 3]
    PB = [4, 5, 6, 7]

    b_xT = B("xT_hbm")
    b_out_hbm = B("out_hbm")
    b_w_hbm = {n: B(n + "_hbm") for n in w_h}

    ktm = qs
    vtm = hs

    sch.dma("sp", "dsp", lambda e: e.dma_start(biasc[:, :], biasc_h[:, :]),
            writes=[b_biasc])
    sch.dma("pool", "dpool", lambda e: e.dma_start(browkb[:, :], browk_h[:, :]),
            writes=[b_browkb])
    sch.dma("pool", "dpool", lambda e: e.dma_start(browvb[:, :], browv_h[:, :]),
            writes=[b_browvb])
    sch.op("pool", lambda e: e.memset(ones[:, :], 1.0), writes=[b_ones])
    sch.op("pool", lambda e: e.memset(epsc[:, :], EPS), writes=[b_epsc])
    sch.op("pool", lambda e: e.memset(neghalf[:, :], -0.5), writes=[b_neghalf])
    sch.op("pool", lambda e: e.memset(negone[:, :], -1.0), writes=[b_negone])
    sch.op("pool", lambda e: e.memset(zeroc[:, :], 0.0), writes=[b_zeroc])
    sch.op("pool", lambda e: e.memset(junkt[:, :], 1.0), writes=[b_junkt])
    sch.op("pool", lambda e: e.memset(kvblk[:, :, :], 0.0), writes=[b_kvblk])
    sch.op("pool", lambda e: e.memset(e2m[:, :, :], 0.0), writes=[b_e2m])
    sch.op("pool", lambda e: e.affine_select(
        out=e2m[:, :, :].rearrange("j p (h m) -> j p h m", h=2),
        in_=e2m[:, :, :].rearrange("j p (h m) -> j p h m", h=2),
        compare_op=ALU.not_equal, fill=1.0, base=0,
        pattern=[[-2, C], [-1, 2], [0, 64]], channel_multiplier=1),
        reads=[b_e2m], writes=[b_e2m])
    sch.op("dve", lambda e: e.tensor_tensor(bhb[:, :], biasc[:, 8:16], biasc[:, 16:24], ALU.add),
           reads=[b_biasc], writes=[b_bhb])

    def dma_x(i):
        buf = xbf[i % 2]
        bb = b_xbf[i % 2]
        src = xT[:, i * T:(i + 1) * T].rearrange("(c p) t -> p c t", p=128)
        sch.dma("pool", "dpool",
                lambda e, s=src, d=buf: e.dma_start(d[:, :, :], s),
                reads=[b_xT], writes=[bb])

    def dma_w(name, slot):
        src = w_h[name][:, :].rearrange("(c p) o -> p c o", p=128)
        sch.dma("pool", "dpool",
                lambda e, s=src, d=wsl[slot]: e.dma_start(d[:, :, :], s),
                reads=[b_w_hbm[name]], writes=[wbuf[slot]])

    def junk(n, bank=0):
        for _ in range(n):
            sch.op("pe", lambda e, bank=bank: e.matmul(
                pk[bank][:, :], ones[:, :], junkt[:, :], start=True, stop=True),
                reads=[b_ones, b_junkt], writes=[b_pk[bank]])

    def emit_xn(i):
        xb_, bxb_ = xbf[i % 2], b_xbf[i % 2]
        xnt, bxnt = xns[i % 2], b_xns[i % 2]
        for c in range(C):
            sch.op("dve", lambda e, c=c, xb_=xb_, i=i: e.tensor_tensor(
                tb1[:, :], xb_[:, c, :], A1[:, i, :], ALU.mult),
                reads=[bxb_, b_A1], writes=[b_tb1])
            sch.op("dve", lambda e, c=c, i=i, xnt=xnt: e.tensor_tensor(
                xnt[:, c, :], tb1[:, :], B1[:, i, :], ALU.subtract),
                reads=[b_tb1, b_B1], writes=[bxnt[c]])

    def emit_stats(datah, b_data, sb0, sb1, sqeng="act"):
        if sqeng == "act":
            for c in range(0, C, 2):
                sch.op("act", lambda e, c=c: e.activation(
                    nsb[:, c:c + 2, :], datah[:, c:c + 2, :], AF.Square),
                    reads=[b_data[c], b_data[c + 1]], writes=[b_nsb[c], b_nsb[c + 1]])
        else:
            for c in range(C - 2):
                sch.op("pool", lambda e, c=c: e.tensor_tensor(
                    nsb[:, c, :], datah[:, c, :], datah[:, c, :], ALU.mult),
                    reads=[b_data[c]], writes=[b_nsb[c]])
            sch.op("act", lambda e: e.activation(
                nsb[:, C - 2:C, :], datah[:, C - 2:C, :], AF.Square),
                reads=[b_data[C - 2], b_data[C - 1]],
                writes=[b_nsb[C - 2], b_nsb[C - 1]])
        for h in range(2):
            for c in range(h * 4, h * 4 + 4):
                sch.op("pe", lambda e, c=c: e.matmul(pk[sb0][:, :], ones[:, :], datah[:, c, :],
                                                     start=(c == 0), stop=(c == C - 1)),
                       reads=[b_ones, b_data[c]], writes=[b_pk[sb0]])
            for c in range(h * 4, h * 4 + 4):
                sch.op("pe", lambda e, c=c: e.matmul(pk[sb1][:, :], ones[:, :], nsb[:, c, :],
                                                     start=(c == 0), stop=(c == C - 1)),
                       reads=[b_ones, b_nsb[c]], writes=[b_pk[sb1]])

    def emit_chain(Aap, b_A, Bap, b_B, sb0, sb1):
        sch.op("dve", lambda e: e.tensor_scalar(m1[:, :], pk[sb0][:, :], 1.0 / DM, None, ALU.mult),
               reads=[b_pk[sb0]], writes=[b_m1])
        sch.op("dve", lambda e: e.tensor_tensor(m2[:, :], m1[:, :], m1[:, :], ALU.mult),
               reads=[b_m1], writes=[b_m2])
        sch.op("dve", lambda e: e.scalar_tensor_tensor(m3[:, :], pk[sb1][:, :], 1.0 / DM, m2[:, :],
                                                       ALU.mult, ALU.subtract),
               reads=[b_pk[sb1], b_m2], writes=[b_m3])
        sch.op("act", lambda e: e.activation(m2[:, :], m3[:, :], AF.Ln, bias=epsc[:, 0:1]),
               reads=[b_m3, b_epsc], writes=[b_m2])
        sch.op("act", lambda e: e.activation(Aap, m2[:, :], AF.Exp, scale=neghalf[:, 0:1]),
               reads=[b_m2, b_neghalf], writes=[b_A])
        sch.op("dve", lambda e: e.tensor_tensor(Bap, m1[:, :], Aap, ALU.mult),
               reads=[b_m1, b_A], writes=[b_B])

    def emit_ln1(j):
        xbj, bxbj = xbf[j % 2], b_xbf[j % 2]
        for c in range(C):
            sch.op("act", lambda e, c=c, xbj=xbj: e.activation(
                nsb[:, c, :], xbj[:, c, :], AF.Square),
                reads=[bxbj], writes=[b_nsb[c]])
        for c in range(C):
            sch.op("pe", lambda e, c=c, xbj=xbj: e.matmul(
                pk[7][:, :], ones[:, :], xbj[:, c, :],
                start=(c == 0), stop=(c == C - 1)),
                reads=[b_ones, bxbj], writes=[b_pk[7]])
        sch.op("dve", lambda e: e.tensor_scalar(m1[:, :], pk[7][:, :], 1.0 / DM, None, ALU.mult),
               reads=[b_pk[7]], writes=[b_m1])
        sch.op("dve", lambda e: e.tensor_tensor(m2[:, :], m1[:, :], m1[:, :], ALU.mult),
               reads=[b_m1], writes=[b_m2])
        for c in range(C):
            sch.op("pe", lambda e, c=c: e.matmul(
                pk[7][:, :], ones[:, :], nsb[:, c, :],
                start=(c == 0), stop=(c == C - 1)),
                reads=[b_ones, b_nsb[c]], writes=[b_pk[7]])
        sch.op("dve", lambda e: e.scalar_tensor_tensor(m3[:, :], pk[7][:, :], 1.0 / DM, m2[:, :],
                                                       ALU.mult, ALU.subtract),
               reads=[b_pk[7], b_m2], writes=[b_m3])
        sch.op("act", lambda e: e.activation(m2[:, :], m3[:, :], AF.Ln, bias=epsc[:, 0:1]),
               reads=[b_m3, b_epsc], writes=[b_m2])
        sch.op("act", lambda e, j=j: e.activation(A1[:, j, :], m2[:, :], AF.Exp,
                                                  scale=neghalf[:, 0:1]),
               reads=[b_m2, b_neghalf], writes=[b_A1])
        sch.op("dve", lambda e, j=j: e.tensor_tensor(B1[:, j, :], m1[:, :], A1[:, j, :], ALU.mult),
               reads=[b_m1, b_A1], writes=[b_B1])

    dma_x(0)
    dma_w("wk", 0)
    dma_x(1)
    dma_w("wv", 1)
    dma_w("wq", 2)
    dma_w("wh", 3)
    for wup in range(24):
        sch.op("pe", lambda e: e.matmul(pk[0][:, 0:128], ones[:, :], ones[:, :],
                                        start=True, stop=True),
               reads=[b_ones], writes=[b_pk[0]])
    emit_ln1(0)
    emit_xn(0)
    dma_x(2)
    for i in range(NT):
        if i + 1 < NT:
            junk(5)
            emit_ln1(i + 1)
            emit_xn(i + 1)
        if i + 3 < NT:
            dma_x(i + 3)
        xn, b_xn = xns[i % 2], b_xns[i % 2]
        for s in range(S):
            for qi in range(4):
                wi = 0 if qi < 2 else 1
                half = qi % 2
                bank = qi
                for kc in range(C):
                    lhs = xn[:, kc, s * 128:(s + 1) * 128]
                    is_k = qi < 2
                    sch.op("pe", lambda e, l=lhs, kc=kc, wi=wi, half=half, bank=bank, is_k=is_k:
                           e.matmul(pk[bank][:, :], l,
                                    wsl[wi][:, kc, half * 512:(half + 1) * 512],
                                    start=(kc == 0), stop=(kc == C - 1 and not is_k)),
                           reads=[b_xn[kc], wbuf[wi]], writes=[b_pk[bank]])
                cu = s * 2 + half
                if qi < 2:
                    sch.op("pe", lambda e, half=half, bank=bank: e.matmul(
                        pk[bank][:, :], ones[0:1, :], browkb[0:1, half * 512:(half + 1) * 512],
                        start=False, stop=True),
                        reads=[b_ones, b_browkb], writes=[b_pk[bank]])
                    sch.op("act", lambda e, cu=cu, bank=bank: e.activation(
                        ktm[:, cu, :], pk[bank][:, :], AF.Sigmoid),
                        reads=[b_pk[bank]], writes=[b_q[cu]])
                else:
                    sch.op("act", lambda e, cu=cu, bank=bank: e.copy(
                        vtm[:, cu, :], pk[bank][:, :]),
                        reads=[b_pk[bank]], writes=[b_h[cu]])
            first = (i == 0 and s == 0)
            last = (i == NT - 1 and s == S - 1)
            for pr in range(8):
                bank = 4 if pr < 4 else 5
                col = (pr % 4) * 128
                cu = s * 2 + pr // 4
                sch.op("pe", lambda e, cu=cu, pr=pr, bank=bank, col=col, first=first, last=last:
                       e.matmul(pk[bank][:, col:col + 128],
                                ktm[:, cu, (pr % 4) * 128:(pr % 4 + 1) * 128],
                                vtm[:, cu, (pr % 4) * 128:(pr % 4 + 1) * 128],
                                start=first, stop=last, skip_group_check=True),
                       reads=[b_q[cu], b_h[cu]], writes=[b_pk[bank]])
                sch.op("pe", lambda e, cu=cu, pr=pr, first=first, last=last:
                       e.matmul(pk[6][:, pr:pr + 1],
                                ktm[:, cu, (pr % 4) * 128:(pr % 4 + 1) * 128],
                                ones[:, 0:1], start=first, stop=last, skip_group_check=True),
                       reads=[b_q[cu], b_ones], writes=[b_pk[6]])

    dma_x(0)
    dma_w("wo", 0)
    dma_w("wf", 1)
    dma_x(1)
    emit_xn(0)

    junk(30, bank=1)
    sch.op("act", lambda e: e.copy(ksum_sb[:, :], pk[6][:, 0:C]),
           reads=[b_pk[6]], writes=[b_ksum])
    for pr in range(8):
        bank = 4 if pr < 4 else 5
        col = (pr % 4) * 128
        sch.op("act", lambda e, pr=pr, bank=bank, col=col: e.copy(
            kv_sb[0:64, pr, :], pk[bank][0:64, col:col + 64]),
            reads=[b_pk[bank]], writes=[b_kv])
        sch.op("act", lambda e, pr=pr, bank=bank, col=col: e.copy(
            kv_sb[64:128, pr, :], pk[bank][64:128, col + 64:col + 128]),
            reads=[b_pk[bank]], writes=[b_kv])
    for pr in range(8):
        sch.op("pe", lambda e, pr=pr: e.matmul(pk[0][0:64, 0:64], ones[0:1, 0:64],
                                               browvb[0:1, pr * 128:pr * 128 + 64],
                                               start=True, stop=True),
               reads=[b_ones, b_browvb], writes=[b_pk[0]])
        sch.op("pe", lambda e, pr=pr: e.matmul(pk[0][64:128, 0:64], ones[0:1, 0:64],
                                               browvb[0:1, pr * 128 + 64:pr * 128 + 128],
                                               start=True, stop=True),
               reads=[b_ones, b_browvb], writes=[b_pk[0]])
        sch.op("act", lambda e, pr=pr: e.copy(bvband[:, pr, :], pk[0][:, 0:64]),
               reads=[b_pk[0]], writes=[b_bvband])
    sch.op("dve", lambda e: e.memset(ltden[:, :, :], 0.0), writes=[b_ltden])
    for pr in range(8):
        sch.op("dve", lambda e, pr=pr: e.scalar_tensor_tensor(
            kvblk[0:64, pr, 0:64], bvband[0:64, pr, :], ksum_sb[0:64, pr:pr + 1],
            kv_sb[0:64, pr, :], ALU.mult, ALU.add),
            reads=[b_bvband, b_ksum, b_kv], writes=[b_kvblk])
        sch.op("dve", lambda e, pr=pr: e.scalar_tensor_tensor(
            kvblk[64:128, pr, 64:128], bvband[64:128, pr, :], ksum_sb[64:128, pr:pr + 1],
            kv_sb[64:128, pr, :], ALU.mult, ALU.add),
            reads=[b_bvband, b_ksum, b_kv], writes=[b_kvblk])
        sch.op("dve", lambda e, pr=pr: e.tensor_copy(
            ltden[0:64, pr, 2 * pr:2 * pr + 1], ksum_sb[0:64, pr:pr + 1]),
            reads=[b_ksum], writes=[b_ltden])
        sch.op("dve", lambda e, pr=pr: e.tensor_copy(
            ltden[64:128, pr, 2 * pr + 1:2 * pr + 2], ksum_sb[64:128, pr:pr + 1]),
            reads=[b_ksum], writes=[b_ltden])

    def proj_mms(wslot, bw, rhs, b_rhs, g):
        banks = PA if g == 0 else PB
        for kc in range(C):
            for j in range(4):
                oc = g * 4 + j
                bk = banks[j]
                sch.op("pe", lambda e, kc=kc, oc=oc, bk=bk, w=wslot: e.matmul(
                    pk[bk][:, :], w[:, kc, oc * 128:(oc + 1) * 128], rhs[:, kc, :],
                    start=(kc == 0), stop=(kc == C - 1)),
                    reads=[bw, b_rhs[kc]], writes=[b_pk[bk]])

    def proj_evacs(g, evac):
        banks = PA if g == 0 else PB
        for j in range(4):
            oc = g * 4 + j
            evac(oc, pk[banks[j]], b_pk[banks[j]])

    def proj(wslot, bw, rhs, b_rhs, evac):
        for g in range(2):
            proj_mms(wslot, bw, rhs, b_rhs, g)
            proj_evacs(g, evac)

    def ev_q(oc, p, bp):
        sch.op("act", lambda e, oc=oc, p=p: e.activation(
            qs[:, oc, :], p[:, :], AF.Sigmoid, bias=biasc[:, oc:oc + 1]),
            reads=[bp, b_biasc], writes=[b_q[oc]])

    def emit_den():
        for c in range(C):
            sch.op("pe", lambda e, c=c: e.matmul(
                pk[4][0:16, :], ltden[:, c, :], qs[:, c, :],
                start=(c == 0), stop=(c == C - 1)),
                reads=[b_ltden, b_q[c]], writes=[b_pk[4]])
        sch.op("act", lambda e: e.activation(m1[0:16, :], pk[4][0:16, :], AF.Ln),
               reads=[b_pk[4]], writes=[b_m1])
        sch.op("act", lambda e: e.activation(dinv16[0:16, :], m1[0:16, :], AF.Exp,
                                             scale=negone[0:16, 0:1]),
               reads=[b_m1, b_negone], writes=[b_dinv16])

    xn0, b_xn0 = xns[0], b_xns[0]
    proj(wsl[2], wbuf[2], xn0, b_xn0, ev_q)
    emit_den()

    for i in range(NT):
        xb, bxb = xbf[i % 2], b_xbf[i % 2]
        xn, b_xn = xns[i % 2], b_xns[i % 2]

        def ev_h(oc, p, bp):
            sch.op("act", lambda e, oc=oc, p=p: e.activation(
                hs[:, oc, :], p[:, :], AF.Identity, bias=bhb[:, oc:oc + 1]),
                reads=[bp, b_bhb], writes=[b_h[oc]])
        proj(wsl[3], wbuf[3], xn, b_xn, ev_h)

        for g in range(2):
            for pr in range(g * 4, g * 4 + 4):
                eb = 4 + ((pr + 1) % 4)
                sch.op("pe", lambda e, pr=pr, eb=eb: e.matmul(
                    pk[eb][:, :], e2m[0:16, pr, :], dinv16[0:16, :], start=True, stop=True),
                    reads=[b_e2m, b_dinv16], writes=[b_pk[eb]])
            for pr in range(g * 4, g * 4 + 4):
                nb = PA[pr % 4]
                sch.op("pe", lambda e, pr=pr, nb=nb: e.matmul(
                    pk[nb][:, :], kvblk[:, pr, :], qs[:, pr, :], start=True, stop=True),
                    reads=[b_kvblk, b_q[pr]], writes=[b_pk[nb]])
            for pr in range(g * 4, g * 4 + 4):
                eb = 4 + ((pr + 1) % 4)
                sch.op("act", lambda e, pr=pr, eb=eb: e.activation(
                    nsb[:, pr, :], pk[eb][:, :], AF.Identity, bias=zeroc[:, 0:1]),
                    reads=[b_pk[eb], b_zeroc], writes=[b_nsb[pr]])
            for pr in range(g * 4, g * 4 + 4):
                nb = PA[pr % 4]
                sch.op("dve", lambda e, pr=pr, nb=nb: e.tensor_tensor(
                    s1[:, pr, :], pk[nb][:, :], nsb[:, pr, :], ALU.mult),
                    reads=[b_pk[nb], b_nsb[pr]], writes=[b_s1[pr]])

        junk(6)
        emit_stats(s1, b_s1, 6, 7)
        emit_chain(reps["A2"][:, :], b_reps["A2"], reps["B2"][:, :], b_reps["B2"], 6, 7)
        junk(20)
        for c in range(C):
            sch.op("dve", lambda e, c=c: e.tensor_tensor(
                s2[:, c, :], s1[:, c, :], reps["A2"][:, :], ALU.mult),
                reads=[b_s1[c], b_reps["A2"]], writes=[b_s2[c]])
            sch.op("dve", lambda e, c=c: e.tensor_tensor(
                s1[:, c, :], s2[:, c, :], reps["B2"][:, :], ALU.subtract),
                reads=[b_s2[c], b_reps["B2"]], writes=[b_s1[c]])
            sch.op("dve", lambda e, c=c: e.tensor_scalar(
                s2[:, c, :], s1[:, c, :], biasc[:, 24 + c:25 + c], biasc[:, 32 + c:33 + c],
                ALU.mult, ALU.add),
                reads=[b_s1[c], b_biasc], writes=[b_s2[c]])
            sch.op("dve", lambda e, c=c: e.tensor_tensor(
                s1[:, c, :], s2[:, c, :], hs[:, c, :], ALU.mult),
                reads=[b_s2[c], b_h[c]], writes=[b_s1[c]])

        if i + 1 < NT:
            emit_xn(i + 1)

        def ev_ao(oc, p, bp):
            sch.op("act", lambda e, oc=oc, p=p: e.activation(
                s2[:, oc, :], p[:, :], AF.Relu, bias=biasc[:, 40 + oc:41 + oc]),
                reads=[bp, b_biasc], writes=[b_s2[oc]])
        proj(wsl[0], wbuf[0], s1, b_s1, ev_ao)

        emit_stats(s2, b_s2, 6, 7, sqeng="pool")
        emit_chain(reps["A3"][:, :], b_reps["A3"], reps["B3"][:, :], b_reps["B3"], 6, 7)
        junk(20)
        for c in range(C):
            sch.op("dve", lambda e, c=c: e.tensor_tensor(
                s1[:, c, :], s2[:, c, :], reps["A3"][:, :], ALU.mult),
                reads=[b_s2[c], b_reps["A3"]], writes=[b_s1[c]])
            sch.op("dve", lambda e, c=c, xb=xb: e.tensor_tensor(
                YY[:, c, :], s1[:, c, :], xb[:, c, :], ALU.add),
                reads=[b_s1[c], bxb], writes=[b_YY[c]])

        if i + 2 < NT:
            dma_x(i + 2)

        emit_stats(YY, b_YY, 6, 7)
        emit_chain(reps["A4"][:, :], b_reps["A4"], reps["B4"][:, :], b_reps["B4"], 6, 7)
        junk(20)
        for c in range(C):
            sch.op("dve", lambda e, c=c: e.tensor_tensor(
                s1[:, c, :], YY[:, c, :], reps["A4"][:, :], ALU.mult),
                reads=[b_YY[c], b_reps["A4"]], writes=[b_s1[c]])
            sch.op("dve", lambda e, c=c: e.tensor_tensor(
                s2[:, c, :], s1[:, c, :], reps["B4"][:, :], ALU.subtract),
                reads=[b_s1[c], b_reps["B4"]], writes=[b_s2[c]])

        def ev_f(oc, p, bp):
            sch.op("act", lambda e, oc=oc, p=p: e.activation(
                s1[:, oc, :], p[:, :], AF.Identity, bias=biasc[:, 48 + oc:49 + oc]),
                reads=[bp, b_biasc], writes=[b_s1[oc]])
        proj(wsl[1], wbuf[1], s2, b_s2, ev_f)

        if i + 1 < NT:
            proj_mms(wsl[2], wbuf[2], xns[(i + 1) % 2], b_xns[(i + 1) % 2], 0)
        emit_stats(s1, b_s1, 6, 7, sqeng=("pool" if i < NT - 1 else "act"))
        emit_chain(reps["A5"][:, :], b_reps["A5"], reps["B5"][:, :], b_reps["B5"], 6, 7)
        if i + 1 < NT:
            proj_mms(wsl[2], wbuf[2], xns[(i + 1) % 2], b_xns[(i + 1) % 2], 1)
            proj_evacs(0, ev_q)
            proj_evacs(1, ev_q)
            emit_den()
        sch.op("dve", lambda e: e.tensor_tensor(
            reps["C5"][:, :], reps["B5"][:, :], reps["B3"][:, :], ALU.add),
            reads=[b_reps["B5"], b_reps["B3"]], writes=[b_reps["C5"]])

        if i < NT - 1:
            for half in range(2):
                for cc in range(4):
                    c = half * 4 + cc
                    sch.op("pool", lambda e, c=c: e.tensor_tensor(
                        tp1[:, :], s1[:, c, :], reps["A5"][:, :], ALU.mult),
                        reads=[b_s1[c], b_reps["A5"]], writes=[b_tp1])
                    sch.op("pool", lambda e, c=c: e.tensor_tensor(
                        tp2[:, :], tp1[:, :], reps["C5"][:, :], ALU.subtract),
                        reads=[b_tp1, b_reps["C5"]], writes=[b_tp2])
                    sch.op("pool", lambda e, c=c, cc=cc: e.tensor_tensor(
                        out_st[:, cc, :], tp2[:, :], YY[:, c, :], ALU.add),
                        reads=[b_tp2, b_YY[c]], writes=[b_outs[cc]])
                dst = outT[half * 512:(half + 1) * 512, i * T:(i + 1) * T].rearrange(
                    "(c p) t -> p c t", p=128)
                sch.dma("sp", "dsp", lambda e, d=dst: e.dma_start(d, out_st[:, :, :]),
                        reads=[b_outs[0], b_outs[1], b_outs[2], b_outs[3]],
                        writes=[b_out_hbm])
        else:
            slots = [(out_st[:, j, :], b_outs[j]) for j in range(4)]
            slots += [(m1[:, :], b_m1), (m2[:, :], b_m2), (m3[:, :], b_m3),
                      (out_st[:, 0, :], b_outs[0])]
            engs = ["dve", "pool", "dve", "dve", "pool", "dve", "dve", "pool"]
            scr = {"dve": (tp1, b_tp1, tp2, b_tp2), "pool": (tb1, b_tb1, tb2, b_tb2)}
            for c in range(C):
                ap, bslot = slots[c]
                eng = engs[c]
                t1, bt1, t2, bt2 = scr[eng]
                sch.op(eng, lambda e, c=c, t1=t1: e.tensor_tensor(
                    t1[:, :], s1[:, c, :], reps["A5"][:, :], ALU.mult),
                    reads=[b_s1[c], b_reps["A5"]], writes=[bt1])
                sch.op(eng, lambda e, t1=t1, t2=t2: e.tensor_tensor(
                    t2[:, :], t1[:, :], reps["C5"][:, :], ALU.subtract),
                    reads=[bt1, b_reps["C5"]], writes=[bt2])
                sch.op(eng, lambda e, c=c, ap=ap, t2=t2: e.tensor_tensor(
                    ap, t2[:, :], YY[:, c, :], ALU.add),
                    reads=[bt2, b_YY[c]], writes=[bslot])
                dst = outT[c * 128:(c + 1) * 128, i * T:(i + 1) * T]
                q, qs_ = ("sp", "dsp") if c % 2 == 0 else ("act", "dact")
                sch.dma(q, qs_, lambda e, d=dst, ap=ap: e.dma_start(d, ap),
                        reads=[bslot], writes=[b_out_hbm])

    junk(60, bank=2)

    sch.finalize()
    total_dsp = sch.dma_cnt.get("dsp", 0)
    total_dact = sch.dma_cnt.get("dact", 0)

    with (
        nc.semaphore("s_pe") as s_pe,
        nc.semaphore("s_act") as s_act,
        nc.semaphore("s_dve") as s_dve,
        nc.semaphore("s_pool") as s_pool,
        nc.semaphore("s_sp") as s_sp,
        nc.semaphore("d_sp") as d_sp,
        nc.semaphore("d_pool") as d_pool,
        nc.semaphore("d_act") as d_act,
        nc.Block() as block,
    ):
        esems = {"pe": s_pe, "act": s_act, "dve": s_dve, "pool": s_pool, "sp": s_sp}
        dsems = {"dsp": d_sp, "dpool": d_pool, "dact": d_act}

        @block.tensor
        def _(eng):
            sch.emit_engine("pe", eng, esems, dsems)

        @block.scalar
        def _(eng):
            sch.emit_engine("act", eng, esems, dsems)

        @block.vector
        def _(eng):
            sch.emit_engine("dve", eng, esems, dsems)

        @block.gpsimd
        def _(eng):
            sch.emit_engine("pool", eng, esems, dsems)

        @block.sync
        def _(eng):
            sch.emit_engine("sp", eng, esems, dsems)
            eng.wait_ge(d_sp, total_dsp)
            if total_dact:
                eng.wait_ge(d_act, total_dact)

    ctx.close()
    return nc


def _perm_cols(W):
    return np.ascontiguousarray(W.reshape(DM, 64, 16).transpose(0, 2, 1).reshape(DM, DM))


def _perm_vec(v):
    return np.ascontiguousarray(v.reshape(64, 16).T.reshape(-1))


def _perm_rows(W):
    return np.ascontiguousarray(W.reshape(64, 16, DM).transpose(1, 0, 2).reshape(DM, DM))


def _cols(v):
    return np.ascontiguousarray(np.asarray(v, np.float32).reshape(8, 128).T)


def kernel(L0_attn_x, L0_attn_Wh, L0_attn_bh, L0_attn_Wq, L0_attn_bq,
           L0_attn_Wk, L0_attn_bk, L0_attn_Wv, L0_attn_bv, L0_attn_beta,
           L0_attn_ln_w, L0_attn_ln_b, L0_attn_Wo, L0_attn_bo,
           L0_ffn_weight, L0_ffn_bias):
    f32 = lambda a: np.ascontiguousarray(np.asarray(a, np.float32))
    x = f32(L0_attn_x)
    wk = _perm_cols(f32(L0_attn_Wk))
    wv = _perm_cols(f32(L0_attn_Wv))
    wq = _perm_cols(f32(L0_attn_Wq))
    wh = _perm_cols(f32(L0_attn_Wh))
    wo = _perm_rows(f32(L0_attn_Wo))
    wf = np.ascontiguousarray(f32(L0_ffn_weight).T)
    biasc = np.concatenate([
        _cols(_perm_vec(f32(L0_attn_bq))),
        _cols(_perm_vec(f32(L0_attn_bh))),
        _cols(_perm_vec(f32(L0_attn_beta))),
        _cols(_perm_vec(f32(L0_attn_ln_w))),
        _cols(_perm_vec(f32(L0_attn_ln_b))),
        _cols(f32(L0_attn_bo)),
        _cols(f32(L0_ffn_bias)),
    ], axis=1)
    biasc = np.ascontiguousarray(biasc)
    browk = np.ascontiguousarray(_perm_vec(f32(L0_attn_bk)).reshape(1, DM))
    browv = np.ascontiguousarray(_perm_vec(f32(L0_attn_bv)).reshape(1, DM))

    nc = build_program()
    in_maps = []
    for c in range(NCORES):
        in_maps.append({
            "xT": np.ascontiguousarray(x[c].T),
            "wk": wk, "wv": wv, "wq": wq, "wh": wh, "wo": wo, "wf": wf,
            "biasc": biasc, "browk": browk, "browv": browv,
        })
    res = run_bass_kernel_spmd(nc, in_maps, core_ids=list(range(NCORES)))
    out = np.stack([np.asarray(res.results[c]["outT"]).T for c in range(NCORES)])
    return np.ascontiguousarray(out.astype(np.float32))


# version 48 (final)
# speedup vs baseline: 1.0131x; 1.0131x over previous
import numpy as np
import ml_dtypes
import concourse.bass as bass
import concourse.mybir as mybir
from concourse.bass_utils import run_bass_kernel_spmd

F32 = mybir.dt.float32
BF16 = mybir.dt.bfloat16
AF = mybir.ActivationFunctionType
ALU = mybir.AluOpType

NCORES = 8
NTOK = 4096
DM = 1024
NT = 8
T = 512
S = 4
C = 8
EPS = 1e-5

ENGINES = ("pe", "act", "dve", "pool", "sp")


class _Buf:
    __slots__ = ("name", "writer", "readers")

    def __init__(self, name):
        self.name = name
        self.writer = None
        self.readers = []


class Sched:
    def __init__(self):
        self.ops = {e: [] for e in ENGINES}
        self.dma_cnt = {}
        self.need = {e: set() for e in ENGINES}

    def buf(self, name):
        return _Buf(name)

    def bufs(self, name, n=C):
        return [_Buf(f"{name}{i}") for i in range(n)]

    def _waits_for(self, eng, reads, writes):
        ew = {}
        dw = {}

        def add(w):
            if w[0] == "E":
                if w[1] != eng and ew.get(w[1], -1) < w[2]:
                    ew[w[1]] = w[2]
            else:
                if dw.get(w[1], 0) < w[2]:
                    dw[w[1]] = w[2]

        for b in reads:
            if b.writer is not None:
                add(b.writer)
        for b in writes:
            if b.writer is not None:
                add(b.writer)
            for r in b.readers:
                add(r)
        return ([("E", k, v) for k, v in ew.items()]
                + [("D", k, v) for k, v in dw.items()])

    def op(self, eng, emit, reads=(), writes=()):
        reads = tuple(reads)
        writes = tuple(writes)
        idx = len(self.ops[eng])
        waits = self._waits_for(eng, reads, writes)
        self.ops[eng].append({"emit": emit, "waits": waits, "dma": None})
        me = ("E", eng, idx)
        for b in reads:
            b.readers.append(me)
        for b in writes:
            b.writer = me
            b.readers = []
        return idx

    def dma(self, queue, sem, emit, reads=(), writes=()):
        reads = tuple(reads)
        writes = tuple(writes)
        waits = self._waits_for(queue, reads, writes)
        val = self.dma_cnt.get(sem, 0) + 16
        self.dma_cnt[sem] = val
        self.ops[queue].append({"emit": emit, "waits": waits, "dma": (sem, val)})
        me = ("D", sem, val)
        for b in reads:
            b.readers.append(me)
        for b in writes:
            b.writer = me
            b.readers = []

    def finalize(self):
        for e in ENGINES:
            for o in self.ops[e]:
                for w in o["waits"]:
                    if w[0] == "E":
                        self.need[w[1]].add(w[2])
        self.valmap = {}
        for e in ENGINES:
            idxs = sorted(self.need[e])
            self.valmap[e] = {ix: r + 1 for r, ix in enumerate(idxs)}

    def emit_engine(self, eng_name, eng, esems, dsems):
        waited = {}
        vm = self.valmap
        my_need = self.need[eng_name]
        for i, o in enumerate(self.ops[eng_name]):
            merged = {}
            for w in o["waits"]:
                if w[0] == "E":
                    key = ("E", w[1])
                    val = vm[w[1]][w[2]]
                else:
                    key = ("D", w[1])
                    val = w[2]
                if merged.get(key, 0) < val:
                    merged[key] = val
            for key, val in merged.items():
                if waited.get(key, 0) >= val:
                    continue
                waited[key] = val
                sem = esems[key[1]] if key[0] == "E" else dsems[key[1]]
                eng.wait_ge(sem, val)
            inst = o["emit"](eng)
            if o["dma"] is not None:
                inst.then_inc(dsems[o["dma"][0]], 16)
            elif i in my_need:
                inst.then_inc(esems[eng_name], 1)


def build_program():
    nc = bass.Bass("TRN2", target_bir_lowering=False)

    xT = nc.dram_tensor("xT", [DM, NTOK], F32, kind="ExternalInput")
    outT = nc.dram_tensor("outT", [DM, NTOK], F32, kind="ExternalOutput")
    w_h = {n: nc.dram_tensor(n, [DM, DM], F32, kind="ExternalInput")
           for n in ("wk", "wv", "wq", "wh", "wo", "wf")}
    biasc_h = nc.dram_tensor("biasc", [128, 56], F32, kind="ExternalInput")
    browk_h = nc.dram_tensor("browk", [1, DM], F32, kind="ExternalInput")
    browv_h = nc.dram_tensor("browv", [1, DM], F32, kind="ExternalInput")

    sch = Sched()
    B = sch.buf
    import contextlib
    ctx = contextlib.ExitStack()

    def sb(name, shape, dt):
        return ctx.enter_context(nc.sbuf_tensor(name, shape, dt))

    def ps(name, shape, dt):
        return ctx.enter_context(nc.psum_tensor(name, shape, dt))

    wsl = [sb(f"w{i}", [128, C, DM], BF16) for i in range(4)]
    wbuf = [B(f"w{i}") for i in range(4)]

    biasc = sb("biascs", [128, 56], F32); b_biasc = B("biasc")
    bhb = sb("bhb", [128, C], F32); b_bhb = B("bhb")
    browkb = sb("browkb", [1, DM], BF16); b_browkb = B("browkb")
    browvb = sb("browvb", [1, DM], BF16); b_browvb = B("browvb")
    ones = sb("ones", [128, 128], BF16); b_ones = B("ones")
    epsc = sb("epsc", [128, 1], F32); b_epsc = B("epsc")
    neghalf = sb("neghalf", [128, 1], F32); b_neghalf = B("neghalf")
    negone = sb("negone", [128, 1], F32); b_negone = B("negone")
    zeroc = sb("zeroc", [128, 1], F32); b_zeroc = B("zeroc")
    junkt = sb("junkt", [128, T], BF16); b_junkt = B("junkt")
    e2m = sb("e2m", [16, C, 128], BF16); b_e2m = B("e2m")
    ltden = sb("ltden", [128, C, 16], BF16); b_ltden = B("ltden")
    dinv16 = sb("dinv16", [16, T], BF16); b_dinv16 = B("dinv16")

    A1 = sb("A1", [128, NT, T], BF16); b_A1 = B("A1")
    B1 = sb("B1", [128, NT, T], BF16); b_B1 = B("B1")
    xbf = [sb(f"xbf{i}", [128, C, T], BF16) for i in range(2)]
    b_xbf = [B("xbf0"), B("xbf1")]

    xns = [sb("xn0", [128, C, T], BF16), sb("xn1", [128, C, T], BF16)]
    b_xns = [sch.bufs("xn0"), sch.bufs("xn1")]
    qs = sb("qs", [128, C, T], BF16); b_q = sch.bufs("q")
    hs = sb("hs", [128, C, T], BF16); b_h = sch.bufs("h")
    nsb = sb("nsb", [128, C, T], BF16); b_nsb = sch.bufs("nsb")
    s1 = sb("s1", [128, C, T], BF16); b_s1 = sch.bufs("s1")
    s2 = sb("s2", [128, C, T], BF16); b_s2 = sch.bufs("s2")
    YY = sb("YY", [128, C, T], BF16); b_YY = sch.bufs("YY")

    kv_sb = sb("kv_sb", [128, C, 64], BF16); b_kv = B("kv_sb")
    kvblk = sb("kvblk", [128, C, 128], BF16); b_kvblk = B("kvblk")
    ksum_sb = sb("ksum_sb", [128, C], F32); b_ksum = B("ksum_sb")
    bvband = sb("bvband", [128, C, 64], BF16); b_bvband = B("bvband")

    reps = {n: sb(n, [128, T], BF16) for n in
            ("A2", "B2", "A3", "B3", "A4", "B4", "A5", "B5", "C5")}
    b_reps = {n: B(n) for n in reps}
    m1 = sb("m1", [128, T], F32); b_m1 = B("m1")
    m2 = sb("m2", [128, T], F32); b_m2 = B("m2")
    m3 = sb("m3", [128, T], F32); b_m3 = B("m3")
    tb1 = sb("tb1", [128, T], BF16); b_tb1 = B("tb1")
    tb2 = sb("tb2", [128, T], BF16); b_tb2 = B("tb2")
    tp1 = sb("tp1", [128, T], BF16); b_tp1 = B("tp1")
    tp2 = sb("tp2", [128, T], BF16); b_tp2 = B("tp2")
    out_st = sb("out_st", [128, 4, T], F32); b_outs = sch.bufs("outst", 4)

    pk = [ps(f"pk{i}", [128, 512], F32) for i in range(8)]
    b_pk = [B(f"pk{i}") for i in range(8)]
    PA = [0, 1, 2, 3]
    PB = [4, 5, 6, 7]

    b_xT = B("xT_hbm")
    b_out_hbm = B("out_hbm")
    b_w_hbm = {n: B(n + "_hbm") for n in w_h}

    ktm = qs
    vtm = hs

    sch.dma("sp", "dsp", lambda e: e.dma_start(biasc[:, :], biasc_h[:, :]),
            writes=[b_biasc])
    sch.dma("pool", "dpool", lambda e: e.dma_start(browkb[:, :], browk_h[:, :]),
            writes=[b_browkb])
    sch.dma("pool", "dpool", lambda e: e.dma_start(browvb[:, :], browv_h[:, :]),
            writes=[b_browvb])
    sch.op("pool", lambda e: e.memset(ones[:, :], 1.0), writes=[b_ones])
    sch.op("pool", lambda e: e.memset(epsc[:, :], EPS), writes=[b_epsc])
    sch.op("pool", lambda e: e.memset(neghalf[:, :], -0.5), writes=[b_neghalf])
    sch.op("pool", lambda e: e.memset(negone[:, :], -1.0), writes=[b_negone])
    sch.op("pool", lambda e: e.memset(zeroc[:, :], 0.0), writes=[b_zeroc])
    sch.op("pool", lambda e: e.memset(junkt[:, :], 1.0), writes=[b_junkt])
    sch.op("pool", lambda e: e.memset(kvblk[:, :, :], 0.0), writes=[b_kvblk])
    sch.op("pool", lambda e: e.memset(e2m[:, :, :], 0.0), writes=[b_e2m])
    sch.op("pool", lambda e: e.affine_select(
        out=e2m[:, :, :].rearrange("j p (h m) -> j p h m", h=2),
        in_=e2m[:, :, :].rearrange("j p (h m) -> j p h m", h=2),
        compare_op=ALU.not_equal, fill=1.0, base=0,
        pattern=[[-2, C], [-1, 2], [0, 64]], channel_multiplier=1),
        reads=[b_e2m], writes=[b_e2m])
    sch.op("dve", lambda e: e.tensor_tensor(bhb[:, :], biasc[:, 8:16], biasc[:, 16:24], ALU.add),
           reads=[b_biasc], writes=[b_bhb])

    def dma_x(i):
        buf = xbf[i % 2]
        bb = b_xbf[i % 2]
        src = xT[:, i * T:(i + 1) * T].rearrange("(c p) t -> p c t", p=128)
        sch.dma("pool", "dpool",
                lambda e, s=src, d=buf: e.dma_start(d[:, :, :], s),
                reads=[b_xT], writes=[bb])

    def dma_w(name, slot):
        src = w_h[name][:, :].rearrange("(c p) o -> p c o", p=128)
        sch.dma("pool", "dpool",
                lambda e, s=src, d=wsl[slot]: e.dma_start(d[:, :, :], s),
                reads=[b_w_hbm[name]], writes=[wbuf[slot]])

    def junk(n, bank=0):
        for _ in range(n):
            sch.op("pe", lambda e, bank=bank: e.matmul(
                pk[bank][:, :], ones[:, :], junkt[:, :], start=True, stop=True),
                reads=[b_ones, b_junkt], writes=[b_pk[bank]])

    def emit_xn(i):
        xb_, bxb_ = xbf[i % 2], b_xbf[i % 2]
        xnt, bxnt = xns[i % 2], b_xns[i % 2]
        for c in range(C):
            sch.op("dve", lambda e, c=c, xb_=xb_, i=i: e.tensor_tensor(
                tb1[:, :], xb_[:, c, :], A1[:, i, :], ALU.mult),
                reads=[bxb_, b_A1], writes=[b_tb1])
            sch.op("dve", lambda e, c=c, i=i, xnt=xnt: e.tensor_tensor(
                xnt[:, c, :], tb1[:, :], B1[:, i, :], ALU.subtract),
                reads=[b_tb1, b_B1], writes=[bxnt[c]])

    def emit_stats(datah, b_data, sb0, sb1, sqeng="act"):
        if sqeng == "act":
            for c in range(0, C, 2):
                sch.op("act", lambda e, c=c: e.activation(
                    nsb[:, c:c + 2, :], datah[:, c:c + 2, :], AF.Square),
                    reads=[b_data[c], b_data[c + 1]], writes=[b_nsb[c], b_nsb[c + 1]])
        else:
            for c in range(C - 2):
                sch.op("pool", lambda e, c=c: e.tensor_tensor(
                    nsb[:, c, :], datah[:, c, :], datah[:, c, :], ALU.mult),
                    reads=[b_data[c]], writes=[b_nsb[c]])
            sch.op("act", lambda e: e.activation(
                nsb[:, C - 2:C, :], datah[:, C - 2:C, :], AF.Square),
                reads=[b_data[C - 2], b_data[C - 1]],
                writes=[b_nsb[C - 2], b_nsb[C - 1]])
        for h in range(2):
            for c in range(h * 4, h * 4 + 4):
                sch.op("pe", lambda e, c=c: e.matmul(pk[sb0][:, :], ones[:, :], datah[:, c, :],
                                                     start=(c == 0), stop=(c == C - 1)),
                       reads=[b_ones, b_data[c]], writes=[b_pk[sb0]])
            for c in range(h * 4, h * 4 + 4):
                sch.op("pe", lambda e, c=c: e.matmul(pk[sb1][:, :], ones[:, :], nsb[:, c, :],
                                                     start=(c == 0), stop=(c == C - 1)),
                       reads=[b_ones, b_nsb[c]], writes=[b_pk[sb1]])

    def emit_chain(Aap, b_A, Bap, b_B, sb0, sb1):
        sch.op("dve", lambda e: e.tensor_scalar(m1[:, :], pk[sb0][:, :], 1.0 / DM, None, ALU.mult),
               reads=[b_pk[sb0]], writes=[b_m1])
        sch.op("dve", lambda e: e.tensor_tensor(m2[:, :], m1[:, :], m1[:, :], ALU.mult),
               reads=[b_m1], writes=[b_m2])
        sch.op("dve", lambda e: e.scalar_tensor_tensor(m3[:, :], pk[sb1][:, :], 1.0 / DM, m2[:, :],
                                                       ALU.mult, ALU.subtract),
               reads=[b_pk[sb1], b_m2], writes=[b_m3])
        sch.op("act", lambda e: e.activation(m2[:, :], m3[:, :], AF.Ln, bias=epsc[:, 0:1]),
               reads=[b_m3, b_epsc], writes=[b_m2])
        sch.op("act", lambda e: e.activation(Aap, m2[:, :], AF.Exp, scale=neghalf[:, 0:1]),
               reads=[b_m2, b_neghalf], writes=[b_A])
        sch.op("dve", lambda e: e.tensor_tensor(Bap, m1[:, :], Aap, ALU.mult),
               reads=[b_m1, b_A], writes=[b_B])

    def emit_ln1(j):
        xbj, bxbj = xbf[j % 2], b_xbf[j % 2]
        for c in range(C):
            sch.op("act", lambda e, c=c, xbj=xbj: e.activation(
                nsb[:, c, :], xbj[:, c, :], AF.Square),
                reads=[bxbj], writes=[b_nsb[c]])
        for c in range(C):
            sch.op("pe", lambda e, c=c, xbj=xbj: e.matmul(
                pk[7][:, :], ones[:, :], xbj[:, c, :],
                start=(c == 0), stop=(c == C - 1)),
                reads=[b_ones, bxbj], writes=[b_pk[7]])
        sch.op("dve", lambda e: e.tensor_scalar(m1[:, :], pk[7][:, :], 1.0 / DM, None, ALU.mult),
               reads=[b_pk[7]], writes=[b_m1])
        sch.op("dve", lambda e: e.tensor_tensor(m2[:, :], m1[:, :], m1[:, :], ALU.mult),
               reads=[b_m1], writes=[b_m2])
        for c in range(C):
            sch.op("pe", lambda e, c=c: e.matmul(
                pk[7][:, :], ones[:, :], nsb[:, c, :],
                start=(c == 0), stop=(c == C - 1)),
                reads=[b_ones, b_nsb[c]], writes=[b_pk[7]])
        sch.op("dve", lambda e: e.scalar_tensor_tensor(m3[:, :], pk[7][:, :], 1.0 / DM, m2[:, :],
                                                       ALU.mult, ALU.subtract),
               reads=[b_pk[7], b_m2], writes=[b_m3])
        sch.op("act", lambda e: e.activation(m2[:, :], m3[:, :], AF.Ln, bias=epsc[:, 0:1]),
               reads=[b_m3, b_epsc], writes=[b_m2])
        sch.op("act", lambda e, j=j: e.activation(A1[:, j, :], m2[:, :], AF.Exp,
                                                  scale=neghalf[:, 0:1]),
               reads=[b_m2, b_neghalf], writes=[b_A1])
        sch.op("dve", lambda e, j=j: e.tensor_tensor(B1[:, j, :], m1[:, :], A1[:, j, :], ALU.mult),
               reads=[b_m1, b_A1], writes=[b_B1])

    dma_x(0)
    dma_w("wk", 0)
    dma_x(1)
    dma_w("wv", 1)
    dma_w("wq", 2)
    dma_w("wh", 3)
    for wup in range(24):
        sch.op("pe", lambda e: e.matmul(pk[0][:, 0:128], ones[:, :], ones[:, :],
                                        start=True, stop=True),
               reads=[b_ones], writes=[b_pk[0]])
    emit_ln1(0)
    emit_xn(0)
    dma_x(2)
    for i in range(NT):
        if i + 1 < NT:
            emit_ln1(i + 1)
            emit_xn(i + 1)
        if i + 3 < NT:
            dma_x(i + 3)
        xn, b_xn = xns[i % 2], b_xns[i % 2]
        for s in range(S):
            for qi in range(4):
                wi = 0 if qi < 2 else 1
                half = qi % 2
                bank = qi
                for kc in range(C):
                    lhs = xn[:, kc, s * 128:(s + 1) * 128]
                    is_k = qi < 2
                    sch.op("pe", lambda e, l=lhs, kc=kc, wi=wi, half=half, bank=bank, is_k=is_k:
                           e.matmul(pk[bank][:, :], l,
                                    wsl[wi][:, kc, half * 512:(half + 1) * 512],
                                    start=(kc == 0), stop=(kc == C - 1 and not is_k)),
                           reads=[b_xn[kc], wbuf[wi]], writes=[b_pk[bank]])
                cu = s * 2 + half
                if qi < 2:
                    sch.op("pe", lambda e, half=half, bank=bank: e.matmul(
                        pk[bank][:, :], ones[0:1, :], browkb[0:1, half * 512:(half + 1) * 512],
                        start=False, stop=True),
                        reads=[b_ones, b_browkb], writes=[b_pk[bank]])
                    sch.op("act", lambda e, cu=cu, bank=bank: e.activation(
                        ktm[:, cu, :], pk[bank][:, :], AF.Sigmoid),
                        reads=[b_pk[bank]], writes=[b_q[cu]])
                else:
                    sch.op("act", lambda e, cu=cu, bank=bank: e.copy(
                        vtm[:, cu, :], pk[bank][:, :]),
                        reads=[b_pk[bank]], writes=[b_h[cu]])
            first = (i == 0 and s == 0)
            last = (i == NT - 1 and s == S - 1)
            for pr in range(8):
                bank = 4 if pr < 4 else 5
                col = (pr % 4) * 128
                cu = s * 2 + pr // 4
                sch.op("pe", lambda e, cu=cu, pr=pr, bank=bank, col=col, first=first, last=last:
                       e.matmul(pk[bank][:, col:col + 128],
                                ktm[:, cu, (pr % 4) * 128:(pr % 4 + 1) * 128],
                                vtm[:, cu, (pr % 4) * 128:(pr % 4 + 1) * 128],
                                start=first, stop=last, skip_group_check=True),
                       reads=[b_q[cu], b_h[cu]], writes=[b_pk[bank]])
                sch.op("pe", lambda e, cu=cu, pr=pr, first=first, last=last:
                       e.matmul(pk[6][:, pr:pr + 1],
                                ktm[:, cu, (pr % 4) * 128:(pr % 4 + 1) * 128],
                                ones[:, 0:1], start=first, stop=last, skip_group_check=True),
                       reads=[b_q[cu], b_ones], writes=[b_pk[6]])

    dma_x(0)
    dma_w("wo", 0)
    dma_w("wf", 1)
    dma_x(1)
    emit_xn(0)

    junk(30, bank=1)
    sch.op("act", lambda e: e.copy(ksum_sb[:, :], pk[6][:, 0:C]),
           reads=[b_pk[6]], writes=[b_ksum])
    for pr in range(8):
        bank = 4 if pr < 4 else 5
        col = (pr % 4) * 128
        sch.op("act", lambda e, pr=pr, bank=bank, col=col: e.copy(
            kv_sb[0:64, pr, :], pk[bank][0:64, col:col + 64]),
            reads=[b_pk[bank]], writes=[b_kv])
        sch.op("act", lambda e, pr=pr, bank=bank, col=col: e.copy(
            kv_sb[64:128, pr, :], pk[bank][64:128, col + 64:col + 128]),
            reads=[b_pk[bank]], writes=[b_kv])
    for pr in range(8):
        sch.op("pe", lambda e, pr=pr: e.matmul(pk[0][0:64, 0:64], ones[0:1, 0:64],
                                               browvb[0:1, pr * 128:pr * 128 + 64],
                                               start=True, stop=True),
               reads=[b_ones, b_browvb], writes=[b_pk[0]])
        sch.op("pe", lambda e, pr=pr: e.matmul(pk[0][64:128, 0:64], ones[0:1, 0:64],
                                               browvb[0:1, pr * 128 + 64:pr * 128 + 128],
                                               start=True, stop=True),
               reads=[b_ones, b_browvb], writes=[b_pk[0]])
        sch.op("act", lambda e, pr=pr: e.copy(bvband[:, pr, :], pk[0][:, 0:64]),
               reads=[b_pk[0]], writes=[b_bvband])
    sch.op("dve", lambda e: e.memset(ltden[:, :, :], 0.0), writes=[b_ltden])
    for pr in range(8):
        sch.op("dve", lambda e, pr=pr: e.scalar_tensor_tensor(
            kvblk[0:64, pr, 0:64], bvband[0:64, pr, :], ksum_sb[0:64, pr:pr + 1],
            kv_sb[0:64, pr, :], ALU.mult, ALU.add),
            reads=[b_bvband, b_ksum, b_kv], writes=[b_kvblk])
        sch.op("dve", lambda e, pr=pr: e.scalar_tensor_tensor(
            kvblk[64:128, pr, 64:128], bvband[64:128, pr, :], ksum_sb[64:128, pr:pr + 1],
            kv_sb[64:128, pr, :], ALU.mult, ALU.add),
            reads=[b_bvband, b_ksum, b_kv], writes=[b_kvblk])
        sch.op("dve", lambda e, pr=pr: e.tensor_copy(
            ltden[0:64, pr, 2 * pr:2 * pr + 1], ksum_sb[0:64, pr:pr + 1]),
            reads=[b_ksum], writes=[b_ltden])
        sch.op("dve", lambda e, pr=pr: e.tensor_copy(
            ltden[64:128, pr, 2 * pr + 1:2 * pr + 2], ksum_sb[64:128, pr:pr + 1]),
            reads=[b_ksum], writes=[b_ltden])

    def proj_mms(wslot, bw, rhs, b_rhs, g):
        banks = PA if g == 0 else PB
        for kc in range(C):
            for j in range(4):
                oc = g * 4 + j
                bk = banks[j]
                sch.op("pe", lambda e, kc=kc, oc=oc, bk=bk, w=wslot: e.matmul(
                    pk[bk][:, :], w[:, kc, oc * 128:(oc + 1) * 128], rhs[:, kc, :],
                    start=(kc == 0), stop=(kc == C - 1)),
                    reads=[bw, b_rhs[kc]], writes=[b_pk[bk]])

    def proj_evacs(g, evac):
        banks = PA if g == 0 else PB
        for j in range(4):
            oc = g * 4 + j
            evac(oc, pk[banks[j]], b_pk[banks[j]])

    def proj(wslot, bw, rhs, b_rhs, evac):
        for g in range(2):
            proj_mms(wslot, bw, rhs, b_rhs, g)
            proj_evacs(g, evac)

    def ev_q(oc, p, bp):
        sch.op("act", lambda e, oc=oc, p=p: e.activation(
            qs[:, oc, :], p[:, :], AF.Sigmoid, bias=biasc[:, oc:oc + 1]),
            reads=[bp, b_biasc], writes=[b_q[oc]])

    def emit_den():
        for c in range(C):
            sch.op("pe", lambda e, c=c: e.matmul(
                pk[4][0:16, :], ltden[:, c, :], qs[:, c, :],
                start=(c == 0), stop=(c == C - 1)),
                reads=[b_ltden, b_q[c]], writes=[b_pk[4]])
        sch.op("act", lambda e: e.activation(m1[0:16, :], pk[4][0:16, :], AF.Ln),
               reads=[b_pk[4]], writes=[b_m1])
        sch.op("act", lambda e: e.activation(dinv16[0:16, :], m1[0:16, :], AF.Exp,
                                             scale=negone[0:16, 0:1]),
               reads=[b_m1, b_negone], writes=[b_dinv16])

    xn0, b_xn0 = xns[0], b_xns[0]
    proj(wsl[2], wbuf[2], xn0, b_xn0, ev_q)
    emit_den()

    for i in range(NT):
        xb, bxb = xbf[i % 2], b_xbf[i % 2]
        xn, b_xn = xns[i % 2], b_xns[i % 2]

        def ev_h(oc, p, bp):
            sch.op("act", lambda e, oc=oc, p=p: e.activation(
                hs[:, oc, :], p[:, :], AF.Identity, bias=bhb[:, oc:oc + 1]),
                reads=[bp, b_bhb], writes=[b_h[oc]])
        proj(wsl[3], wbuf[3], xn, b_xn, ev_h)

        for g in range(2):
            for pr in range(g * 4, g * 4 + 4):
                eb = 4 + ((pr + 1) % 4)
                sch.op("pe", lambda e, pr=pr, eb=eb: e.matmul(
                    pk[eb][:, :], e2m[0:16, pr, :], dinv16[0:16, :], start=True, stop=True),
                    reads=[b_e2m, b_dinv16], writes=[b_pk[eb]])
            for pr in range(g * 4, g * 4 + 4):
                nb = PA[pr % 4]
                sch.op("pe", lambda e, pr=pr, nb=nb: e.matmul(
                    pk[nb][:, :], kvblk[:, pr, :], qs[:, pr, :], start=True, stop=True),
                    reads=[b_kvblk, b_q[pr]], writes=[b_pk[nb]])
            for pr in range(g * 4, g * 4 + 4):
                eb = 4 + ((pr + 1) % 4)
                sch.op("act", lambda e, pr=pr, eb=eb: e.activation(
                    nsb[:, pr, :], pk[eb][:, :], AF.Identity, bias=zeroc[:, 0:1]),
                    reads=[b_pk[eb], b_zeroc], writes=[b_nsb[pr]])
            for pr in range(g * 4, g * 4 + 4):
                nb = PA[pr % 4]
                sch.op("dve", lambda e, pr=pr, nb=nb: e.tensor_tensor(
                    s1[:, pr, :], pk[nb][:, :], nsb[:, pr, :], ALU.mult),
                    reads=[b_pk[nb], b_nsb[pr]], writes=[b_s1[pr]])

        junk(6)
        emit_stats(s1, b_s1, 6, 7)
        emit_chain(reps["A2"][:, :], b_reps["A2"], reps["B2"][:, :], b_reps["B2"], 6, 7)
        junk(20)
        for c in range(C):
            sch.op("dve", lambda e, c=c: e.tensor_tensor(
                s2[:, c, :], s1[:, c, :], reps["A2"][:, :], ALU.mult),
                reads=[b_s1[c], b_reps["A2"]], writes=[b_s2[c]])
            sch.op("dve", lambda e, c=c: e.tensor_tensor(
                s1[:, c, :], s2[:, c, :], reps["B2"][:, :], ALU.subtract),
                reads=[b_s2[c], b_reps["B2"]], writes=[b_s1[c]])
            sch.op("dve", lambda e, c=c: e.tensor_scalar(
                s2[:, c, :], s1[:, c, :], biasc[:, 24 + c:25 + c], biasc[:, 32 + c:33 + c],
                ALU.mult, ALU.add),
                reads=[b_s1[c], b_biasc], writes=[b_s2[c]])
            sch.op("dve", lambda e, c=c: e.tensor_tensor(
                s1[:, c, :], s2[:, c, :], hs[:, c, :], ALU.mult),
                reads=[b_s2[c], b_h[c]], writes=[b_s1[c]])

        if i + 1 < NT:
            emit_xn(i + 1)

        def ev_ao(oc, p, bp):
            sch.op("act", lambda e, oc=oc, p=p: e.activation(
                s2[:, oc, :], p[:, :], AF.Relu, bias=biasc[:, 40 + oc:41 + oc]),
                reads=[bp, b_biasc], writes=[b_s2[oc]])
        proj(wsl[0], wbuf[0], s1, b_s1, ev_ao)

        emit_stats(s2, b_s2, 6, 7, sqeng="pool")
        emit_chain(reps["A3"][:, :], b_reps["A3"], reps["B3"][:, :], b_reps["B3"], 6, 7)
        junk(20)
        for c in range(C):
            sch.op("dve", lambda e, c=c: e.tensor_tensor(
                s1[:, c, :], s2[:, c, :], reps["A3"][:, :], ALU.mult),
                reads=[b_s2[c], b_reps["A3"]], writes=[b_s1[c]])
            sch.op("dve", lambda e, c=c, xb=xb: e.tensor_tensor(
                YY[:, c, :], s1[:, c, :], xb[:, c, :], ALU.add),
                reads=[b_s1[c], bxb], writes=[b_YY[c]])

        if i + 2 < NT:
            dma_x(i + 2)

        emit_stats(YY, b_YY, 6, 7)
        emit_chain(reps["A4"][:, :], b_reps["A4"], reps["B4"][:, :], b_reps["B4"], 6, 7)
        junk(20)
        for c in range(C):
            sch.op("dve", lambda e, c=c: e.tensor_tensor(
                s1[:, c, :], YY[:, c, :], reps["A4"][:, :], ALU.mult),
                reads=[b_YY[c], b_reps["A4"]], writes=[b_s1[c]])
            sch.op("dve", lambda e, c=c: e.tensor_tensor(
                s2[:, c, :], s1[:, c, :], reps["B4"][:, :], ALU.subtract),
                reads=[b_s1[c], b_reps["B4"]], writes=[b_s2[c]])

        def ev_f(oc, p, bp):
            sch.op("act", lambda e, oc=oc, p=p: e.activation(
                s1[:, oc, :], p[:, :], AF.Identity, bias=biasc[:, 48 + oc:49 + oc]),
                reads=[bp, b_biasc], writes=[b_s1[oc]])
        proj(wsl[1], wbuf[1], s2, b_s2, ev_f)

        if i + 1 < NT:
            proj_mms(wsl[2], wbuf[2], xns[(i + 1) % 2], b_xns[(i + 1) % 2], 0)
        emit_stats(s1, b_s1, 6, 7, sqeng=("pool" if i < NT - 1 else "act"))
        emit_chain(reps["A5"][:, :], b_reps["A5"], reps["B5"][:, :], b_reps["B5"], 6, 7)
        if i + 1 < NT:
            proj_mms(wsl[2], wbuf[2], xns[(i + 1) % 2], b_xns[(i + 1) % 2], 1)
            proj_evacs(0, ev_q)
            proj_evacs(1, ev_q)
            emit_den()
        sch.op("dve", lambda e: e.tensor_tensor(
            reps["C5"][:, :], reps["B5"][:, :], reps["B3"][:, :], ALU.add),
            reads=[b_reps["B5"], b_reps["B3"]], writes=[b_reps["C5"]])

        if i < NT - 1:
            for half in range(2):
                for cc in range(4):
                    c = half * 4 + cc
                    sch.op("pool", lambda e, c=c: e.tensor_tensor(
                        tp1[:, :], s1[:, c, :], reps["A5"][:, :], ALU.mult),
                        reads=[b_s1[c], b_reps["A5"]], writes=[b_tp1])
                    sch.op("pool", lambda e, c=c: e.tensor_tensor(
                        tp2[:, :], tp1[:, :], reps["C5"][:, :], ALU.subtract),
                        reads=[b_tp1, b_reps["C5"]], writes=[b_tp2])
                    sch.op("pool", lambda e, c=c, cc=cc: e.tensor_tensor(
                        out_st[:, cc, :], tp2[:, :], YY[:, c, :], ALU.add),
                        reads=[b_tp2, b_YY[c]], writes=[b_outs[cc]])
                dst = outT[half * 512:(half + 1) * 512, i * T:(i + 1) * T].rearrange(
                    "(c p) t -> p c t", p=128)
                sch.dma("sp", "dsp", lambda e, d=dst: e.dma_start(d, out_st[:, :, :]),
                        reads=[b_outs[0], b_outs[1], b_outs[2], b_outs[3]],
                        writes=[b_out_hbm])
        else:
            slots = [(out_st[:, j, :], b_outs[j]) for j in range(4)]
            slots += [(m1[:, :], b_m1), (m2[:, :], b_m2), (m3[:, :], b_m3),
                      (out_st[:, 0, :], b_outs[0])]
            engs = ["dve", "pool", "dve", "dve", "pool", "dve", "dve", "pool"]
            scr = {"dve": (tp1, b_tp1, tp2, b_tp2), "pool": (tb1, b_tb1, tb2, b_tb2)}
            for c in range(C):
                ap, bslot = slots[c]
                eng = engs[c]
                t1, bt1, t2, bt2 = scr[eng]
                sch.op(eng, lambda e, c=c, t1=t1: e.tensor_tensor(
                    t1[:, :], s1[:, c, :], reps["A5"][:, :], ALU.mult),
                    reads=[b_s1[c], b_reps["A5"]], writes=[bt1])
                sch.op(eng, lambda e, t1=t1, t2=t2: e.tensor_tensor(
                    t2[:, :], t1[:, :], reps["C5"][:, :], ALU.subtract),
                    reads=[bt1, b_reps["C5"]], writes=[bt2])
                sch.op(eng, lambda e, c=c, ap=ap, t2=t2: e.tensor_tensor(
                    ap, t2[:, :], YY[:, c, :], ALU.add),
                    reads=[bt2, b_YY[c]], writes=[bslot])
                dst = outT[c * 128:(c + 1) * 128, i * T:(i + 1) * T]
                q, qs_ = ("sp", "dsp") if c % 2 == 0 else ("act", "dact")
                sch.dma(q, qs_, lambda e, d=dst, ap=ap: e.dma_start(d, ap),
                        reads=[bslot], writes=[b_out_hbm])

    junk(60, bank=2)

    sch.finalize()
    total_dsp = sch.dma_cnt.get("dsp", 0)
    total_dact = sch.dma_cnt.get("dact", 0)

    with (
        nc.semaphore("s_pe") as s_pe,
        nc.semaphore("s_act") as s_act,
        nc.semaphore("s_dve") as s_dve,
        nc.semaphore("s_pool") as s_pool,
        nc.semaphore("s_sp") as s_sp,
        nc.semaphore("d_sp") as d_sp,
        nc.semaphore("d_pool") as d_pool,
        nc.semaphore("d_act") as d_act,
        nc.Block() as block,
    ):
        esems = {"pe": s_pe, "act": s_act, "dve": s_dve, "pool": s_pool, "sp": s_sp}
        dsems = {"dsp": d_sp, "dpool": d_pool, "dact": d_act}

        @block.tensor
        def _(eng):
            sch.emit_engine("pe", eng, esems, dsems)

        @block.scalar
        def _(eng):
            sch.emit_engine("act", eng, esems, dsems)

        @block.vector
        def _(eng):
            sch.emit_engine("dve", eng, esems, dsems)

        @block.gpsimd
        def _(eng):
            sch.emit_engine("pool", eng, esems, dsems)

        @block.sync
        def _(eng):
            sch.emit_engine("sp", eng, esems, dsems)
            eng.wait_ge(d_sp, total_dsp)
            if total_dact:
                eng.wait_ge(d_act, total_dact)

    ctx.close()
    return nc


def _perm_cols(W):
    return np.ascontiguousarray(W.reshape(DM, 64, 16).transpose(0, 2, 1).reshape(DM, DM))


def _perm_vec(v):
    return np.ascontiguousarray(v.reshape(64, 16).T.reshape(-1))


def _perm_rows(W):
    return np.ascontiguousarray(W.reshape(64, 16, DM).transpose(1, 0, 2).reshape(DM, DM))


def _cols(v):
    return np.ascontiguousarray(np.asarray(v, np.float32).reshape(8, 128).T)


def kernel(L0_attn_x, L0_attn_Wh, L0_attn_bh, L0_attn_Wq, L0_attn_bq,
           L0_attn_Wk, L0_attn_bk, L0_attn_Wv, L0_attn_bv, L0_attn_beta,
           L0_attn_ln_w, L0_attn_ln_b, L0_attn_Wo, L0_attn_bo,
           L0_ffn_weight, L0_ffn_bias):
    f32 = lambda a: np.ascontiguousarray(np.asarray(a, np.float32))
    x = f32(L0_attn_x)
    wk = _perm_cols(f32(L0_attn_Wk))
    wv = _perm_cols(f32(L0_attn_Wv))
    wq = _perm_cols(f32(L0_attn_Wq))
    wh = _perm_cols(f32(L0_attn_Wh))
    wo = _perm_rows(f32(L0_attn_Wo))
    wf = np.ascontiguousarray(f32(L0_ffn_weight).T)
    biasc = np.concatenate([
        _cols(_perm_vec(f32(L0_attn_bq))),
        _cols(_perm_vec(f32(L0_attn_bh))),
        _cols(_perm_vec(f32(L0_attn_beta))),
        _cols(_perm_vec(f32(L0_attn_ln_w))),
        _cols(_perm_vec(f32(L0_attn_ln_b))),
        _cols(f32(L0_attn_bo)),
        _cols(f32(L0_ffn_bias)),
    ], axis=1)
    biasc = np.ascontiguousarray(biasc)
    browk = np.ascontiguousarray(_perm_vec(f32(L0_attn_bk)).reshape(1, DM))
    browv = np.ascontiguousarray(_perm_vec(f32(L0_attn_bv)).reshape(1, DM))

    nc = build_program()
    in_maps = []
    for c in range(NCORES):
        in_maps.append({
            "xT": np.ascontiguousarray(x[c].T),
            "wk": wk, "wv": wv, "wq": wq, "wh": wh, "wo": wo, "wf": wf,
            "biasc": biasc, "browk": browk, "browv": browv,
        })
    res = run_bass_kernel_spmd(nc, in_maps, core_ids=list(range(NCORES)))
    out = np.stack([np.asarray(res.results[c]["outT"]).T for c in range(NCORES)])
    return np.ascontiguousarray(out.astype(np.float32))
